# version 04
# baseline (speedup 1.0000x reference)
import numpy as np
import ml_dtypes
from contextlib import ExitStack

import concourse.bass as bass
import concourse.mybir as mybir
from concourse.bass_utils import run_bass_kernel_spmd

F32 = mybir.dt.float32
BF16 = mybir.dt.bfloat16
I32 = mybir.dt.int32
AF = mybir.ActivationFunctionType
ALU = mybir.AluOpType
AX = mybir.AxisListType
BF = ml_dtypes.bfloat16

D = 1024
T = 2048
NC = 8
HID = 3072
NJC = 24
LHEADS = 4
GHEADS = 12
NBKT = 16
NU = 32
INV127 = 1.0 / 127.0
EPS = 1e-5

DO_ATTN_A = True
DO_FFN1 = True
DO_ATTN_B = True
DO_FFN2 = True
ATTNA_TAUS = 16
ATTNA_SW = True
DEBUG_H = False
ATTNB_HEADS = 16
ATTNB_PROJ = True
ATTNB_SIG = True
ATTNB_CORE = True
ATTNB_OUT = True
DEBUG_RBC = False
EXTRA_FENCE = True
DEBUG_DELAY = False
DEBUG_PEDRAIN = False
DEBUG_ATTNT = False
DEBUG_STATS = False
DEBUG_LNPAD = False
DEBUG_X2T0 = False
DEBUG_LOAD = False
DEBUG_WOPS = False
DEBUG_STATS_UID = 5
ATTNA_MEMSET = True
ATTNA_WOLOADS = True
ATTNA_LOCAL = True
ATTNA_GLOBAL = True
ATTNA_WO = True


def _build_nc():
    nc = bass.Bass("TRN2", target_bir_lowering=False, debug=False)

    xT = nc.dram_tensor("xT", [D, T], F32, kind="ExternalInput").ap()
    aWq = nc.dram_tensor("aWq", [D, D], F32, kind="ExternalInput").ap()
    aWk = nc.dram_tensor("aWk", [D, D], F32, kind="ExternalInput").ap()
    aWv = nc.dram_tensor("aWv", [D, D], F32, kind="ExternalInput").ap()
    aWo = nc.dram_tensor("aWo", [D, D], F32, kind="ExternalInput").ap()
    abo = nc.dram_tensor("abo", [1, D], F32, kind="ExternalInput").ap()
    bWq = nc.dram_tensor("bWq", [D, D], F32, kind="ExternalInput").ap()
    bWk = nc.dram_tensor("bWk", [D, D], F32, kind="ExternalInput").ap()
    bWv = nc.dram_tensor("bWv", [D, D], F32, kind="ExternalInput").ap()
    bbq = nc.dram_tensor("bbq", [1, D], F32, kind="ExternalInput").ap()
    bbk = nc.dram_tensor("bbk", [1, D], F32, kind="ExternalInput").ap()
    bbv = nc.dram_tensor("bbv", [1, D], F32, kind="ExternalInput").ap()
    btemp = nc.dram_tensor("btemp", [1, 16], F32, kind="ExternalInput").ap()
    wgT = nc.dram_tensor("wgT", [D, HID], I32, kind="ExternalInput").ap()
    wuT = nc.dram_tensor("wuT", [D, HID], I32, kind="ExternalInput").ap()
    wdT = nc.dram_tensor("wdT", [HID, D], I32, kind="ExternalInput").ap()
    ident_in = nc.dram_tensor("ident", [128, 128], BF16, kind="ExternalInput").ap()
    maskN_in = nc.dram_tensor("maskN", [128, 256], BF16, kind="ExternalInput").ap()
    mask0_in = nc.dram_tensor("mask0", [128, 256], BF16, kind="ExternalInput").ap()
    onesc_in = nc.dram_tensor("onesc", [128, 1], BF16, kind="ExternalInput").ap()
    onesr_in = nc.dram_tensor("onesr", [1, 512], BF16, kind="ExternalInput").ap()
    crow_in = nc.dram_tensor("crow", [1, 8], F32, kind="ExternalInput").ap()
    yT = nc.dram_tensor("yT", [D, T], F32, kind="ExternalOutput").ap()
    wg_sl = nc.dram_tensor("wg_sl", [128, 6 * 4096], BF16, kind="Internal").ap()
    wu_sl = nc.dram_tensor("wu_sl", [128, 6 * 4096], BF16, kind="Internal").ap()
    wd_sl = nc.dram_tensor("wd_sl", [128, NJC * 1024], BF16, kind="Internal").ap()

    es = ExitStack()
    with es:
        stream = es.enter_context(nc.sbuf_tensor("stream", [128, NC * T], F32))
        rQ = es.enter_context(nc.sbuf_tensor("rQ", [1, 512], F32))
        rM = es.enter_context(nc.sbuf_tensor("rM", [1, 512], F32))
        rV = es.enter_context(nc.sbuf_tensor("rV", [1, 512], F32))
        rR = es.enter_context(nc.sbuf_tensor("rR", [1, 512], F32))
        r16 = es.enter_context(nc.sbuf_tensor("r16", [1, 512], BF16))
        mrneg = es.enter_context(nc.sbuf_tensor("mrneg", [1, T], BF16))
        temprow = es.enter_context(nc.sbuf_tensor("temprow", [1, 16], BF16))
        rbc = es.enter_context(nc.sbuf_tensor("rbc", [128, T], BF16))
        ident = es.enter_context(nc.sbuf_tensor("identsb", [128, 128], BF16))
        maskN = es.enter_context(nc.sbuf_tensor("maskNsb", [128, 256], BF16))
        mask0 = es.enter_context(nc.sbuf_tensor("mask0sb", [128, 256], BF16))
        onesc = es.enter_context(nc.sbuf_tensor("onescsb", [128, 1], BF16))
        onesr = es.enter_context(nc.sbuf_tensor("onesrsb", [1, 512], BF16))
        crow = es.enter_context(nc.sbuf_tensor("crowsb", [1, 8], F32))
        tempbc = es.enter_context(nc.sbuf_tensor("tempbc", [128, 16], F32))
        xbt = es.enter_context(nc.sbuf_tensor("xbt", [128, 2048], BF16))
        xqt = es.enter_context(nc.sbuf_tensor("xqt", [128, 2048], BF16))
        flushb = es.enter_context(nc.sbuf_tensor("flushb", [1, 8], F32))

        S = {}
        for name in ["ld", "gp", "pe", "dv", "ac", "st"]:
            S[name] = es.enter_context(nc.semaphore(name))
        C = {k: 0 for k in S}

        blk = es.enter_context(nc.Block())

        class _RecInst:
            __slots__ = ("item",)

            def __init__(self, item):
                self.item = item

            def then_inc(self, sem, n=1):
                self.item[3].append((sem, n))
                return self

        class _Rec:
            def __init__(self, name, prog):
                self._name = name
                self._prog = prog

            def __getattr__(self, meth):
                prog, name = self._prog, self._name

                def call(*a, **k):
                    item = (meth, a, k, [])
                    prog[name].append(item)
                    return _RecInst(item)
                return call

        PROG = {n: [] for n in ("sy", "ve", "sc", "te", "gp")}
        sy = _Rec("sy", PROG)
        ve = _Rec("ve", PROG)
        sc = _Rec("sc", PROG)
        te = _Rec("te", PROG)
        gp = _Rec("gp", PROG)

        def _replay(eng, items):
            for meth, a, k, incs in items:
                inst = getattr(eng, meth)(*a, **k)
                for sem, n in incs:
                    inst.then_inc(sem, n)

        def fin(inst, sem, n=1):
            inst.then_inc(S[sem], n)
            C[sem] += n
            return C[sem]

        def ld(out, in_):
            return fin(sy.dma_start(out=out, in_=in_), "ld", 16)

        def ldg(out, in_):
            return fin(gp.dma_start(out=out, in_=in_), "gp", 16)

        clrscope = ExitStack()
        ps_clr = [clrscope.enter_context(nc.psum_tensor("ps_clr%d" % i, [128, 2048], F32))
                  for i in range(2)]
        for i in range(2):
            fin(ve.memset(ps_clr[i][:], 0.0), "dv")
        clrscope.close()

        for c in range(NC):
            ld(stream[:, c * T:(c + 1) * T], xT[c * 128:(c + 1) * 128, :])
        ld(ident[:], ident_in)
        ld(maskN[:], maskN_in)
        ld(mask0[:], mask0_in)
        ld(onesc[:], onesc_in)
        ld(onesr[:], onesr_in)
        v_const = ld(crow[:], crow_in)
        for i, val in enumerate([EPS, 1e-20, -0.5, 0.5, INV127, 1.0 / D]):
            nc.const_aps.register(F32, val, crow[0:1, i:i + 1])

        p1 = ExitStack()
        wq_sb = p1.enter_context(nc.sbuf_tensor("wq_sb", [128, NC * D], BF16))
        wk_sb = p1.enter_context(nc.sbuf_tensor("wk_sb", [128, NC * D], BF16))
        wv_sb = p1.enter_context(nc.sbuf_tensor("wv_sb", [128, NC * D], BF16))
        swrow3 = p1.enter_context(nc.sbuf_tensor("swrow3", [1, 3 * D], BF16))
        borow = p1.enter_context(nc.sbuf_tensor("borow", [1, D], BF16))
        v_wq = v_wk = v_wv = 0
        for c in range(NC):
            v_wq = ldg(wq_sb[:, c * D:(c + 1) * D], aWq[c * 128:(c + 1) * 128, :])
        for c in range(NC):
            v_wk = ldg(wk_sb[:, c * D:(c + 1) * D], aWk[c * 128:(c + 1) * 128, :])
        for c in range(NC):
            v_wv = ldg(wv_sb[:, c * D:(c + 1) * D], aWv[c * 128:(c + 1) * 128, :])
        v_temp = ldg(temprow[:], btemp)
        ldg(borow[:], abo)

        convb = p1.enter_context(nc.sbuf_tensor("convb", [128, 2 * 1536], BF16))
        store_done = {0: 0, 1: 0}
        idx = 0
        conv_done = 0
        conv_jobs = []
        for half in range(2):
            for c in range(NC):
                conv_jobs.append((wgT, wg_sl, c, half))
        for half in range(2):
            for c in range(NC):
                conv_jobs.append((wuT, wu_sl, c, half))
        for c in range(NJC):
            conv_jobs.append((wdT, wd_sl, c, None))
        for src, dst, c, half in conv_jobs:
            sl = idx % 2
            width = 1536 if half is not None else D
            slot = convb[:, sl * 1536:sl * 1536 + width]
            if idx >= 2:
                gp.wait_ge(S["ld"], store_done[sl])
            if half is not None:
                vg = ldg(slot, src[c * 128:(c + 1) * 128, half * 1536:(half + 1) * 1536])
            else:
                vg = ldg(slot, src[c * 128:(c + 1) * 128, :])
            sy.wait_ge(S["gp"], vg)
            if half is not None:
                o = dst.rearrange("p (s q) -> p s q", q=4096)[
                    :, half * 3:half * 3 + 3, c * 512:(c + 1) * 512]
                i = slot.rearrange("p (s q) -> p s q", q=512)
                conv_done = fin(sy.dma_start(out=o, in_=i), "ld", 16)
            else:
                conv_done = fin(sy.dma_start(out=dst[:, c * 1024:(c + 1) * 1024],
                                             in_=slot), "ld", 16)
            store_done[sl] = conv_done
            idx += 1
        v_conv_all = conv_done
        v_conv_gp = C["gp"]

        ln_ctr = [0]

        v_xbt_pe = [0]

        def emit_ln_dve(cols0, n, wave):
            w = n
            ve.wait_ge(S["ld"], v_const)
            if v_xbt_pe[0]:
                ve.wait_ge(S["pe"], v_xbt_pe[0])
            vdv = 0
            cs = wave * 4
            for c in range(cs, cs + 4):
                fin(ve.tensor_copy(out=xbt[:, (c - cs) * w:(c - cs + 1) * w],
                                   in_=stream[:, c * T + cols0:c * T + cols0 + w]), "dv")
            for c in range(cs, cs + 4):
                vdv = fin(ve.tensor_mul(out=xqt[:, (c - cs) * w:(c - cs + 1) * w],
                                        in0=xbt[:, (c - cs) * w:(c - cs + 1) * w],
                                        in1=xbt[:, (c - cs) * w:(c - cs + 1) * w]), "dv")
            return vdv

        def emit_ln_stats_wave(ps_s, ps_q, n, wave, v_dve):
            w = n
            te.wait_ge(S["dv"], v_dve)
            for i in range(4):
                st0 = (wave == 0 and i == 0)
                sp = (wave == 1 and i == 3)
                te.matmul(ps_s[:], onesc[:], xbt[:, i * w:(i + 1) * w],
                          start=st0, stop=sp)
                vpe = fin(te.matmul(ps_q[:], onesc[:], xqt[:, i * w:(i + 1) * w],
                                    start=st0, stop=sp), "pe")
            v_xbt_pe[0] = vpe
            return vpe

        def emit_ln_tail(ps_s, ps_q, sscope, cols0, n, vpe):
            w = n
            sc.wait_ge(S["pe"], vpe)
            fin(sc.activation(out=rM[0:1, 0:w], in_=ps_s[:], func=AF.Copy, scale=1.0 / D), "ac")
            vac = fin(sc.activation(out=rQ[0:1, 0:w], in_=ps_q[:], func=AF.Copy,
                                    scale=1.0 / D), "ac")
            ve.wait_ge(S["ac"], vac)
            fin(ve.tensor_mul(out=rV[0:1, 0:w], in0=rM[0:1, 0:w], in1=rM[0:1, 0:w]), "dv")
            ve.engine_nop(); ve.engine_nop()
            fin(ve.tensor_sub(out=rV[0:1, 0:w], in0=rQ[0:1, 0:w], in1=rV[0:1, 0:w]), "dv")
            ve.engine_nop(); ve.engine_nop()
            vdv = fin(ve.tensor_scalar_add(out=rV[0:1, 0:w], in0=rV[0:1, 0:w],
                                           scalar1=EPS), "dv")
            sc.wait_ge(S["dv"], vdv)
            fin(sc.activation(out=rQ[0:1, 0:w], in_=rV[0:1, 0:w], func=AF.Ln), "ac")
            sc.nop(); sc.nop()
            vac = fin(sc.activation(out=rR[0:1, 0:w], in_=rQ[0:1, 0:w], func=AF.Exp,
                                    scale=-0.5), "ac")
            ve.wait_ge(S["ac"], vac)
            fin(ve.tensor_copy(out=r16[0:1, 0:w], in_=rR[0:1, 0:w]), "dv")
            vdv = fin(ve.scalar_tensor_tensor(out=mrneg[0:1, cols0:cols0 + w],
                                              in0=rM[0:1, 0:w], scalar=-1.0,
                                              in1=rR[0:1, 0:w],
                                              op0=ALU.mult, op1=ALU.mult), "dv")
            sscope.close()
            return vdv

        def emit_ln_psum(n):
            ln_ctr[0] += 1
            sscope = ExitStack()
            ps_s = sscope.enter_context(nc.psum_tensor("ps_s%d" % ln_ctr[0], [1, n], F32))
            ps_q = sscope.enter_context(nc.psum_tensor("ps_q%d" % ln_ctr[0], [1, n], F32))
            return sscope, ps_s, ps_q

        def emit_ln_bc(cols0, n, v_mrneg):
            w = n
            ln_ctr[0] += 1
            bscope = ExitStack()
            ps_bc = bscope.enter_context(nc.psum_tensor("ps_bc%d" % ln_ctr[0], [128, w], F32))
            te.wait_ge(S["dv"], v_mrneg)
            vpe = fin(te.matmul(ps_bc[:], onesr[0:1, 0:128], r16[0:1, 0:w],
                                start=True, stop=True), "pe")
            ve.wait_ge(S["pe"], vpe)
            vdv = fin(ve.tensor_copy(out=rbc[:, cols0:cols0 + w], in_=ps_bc[:]), "dv")
            bscope.close()
            return vdv

        def emit_ln(cols0, n):
            sscope, ps_s, ps_q = emit_ln_psum(n)
            v0 = emit_ln_dve(cols0, n, 0)
            emit_ln_stats_wave(ps_s, ps_q, n, 0, v0)
            v1 = emit_ln_dve(cols0, n, 1)
            vs1 = emit_ln_stats_wave(ps_s, ps_q, n, 1, v1)
            v_mr = emit_ln_tail(ps_s, ps_q, sscope, cols0, n, vs1)
            return emit_ln_bc(cols0, n, v_mr)

        for lb in range(4):
            v_rbc1 = emit_ln(lb * 512, 512)

        if DO_ATTN_A:
            sw_scope = ExitStack()
            ps_sw = sw_scope.enter_context(nc.psum_tensor("ps_sw", [1, 512], F32))
            te.wait_ge(S["gp"], v_wv)
            te.wait_ge(S["dv"], C["dv"])
            te.wait_ge(S["ac"], C["ac"])
            for wi, wsb in enumerate([wq_sb, wk_sb, wv_sb] if ATTNA_SW else []):
                for half in range(2):
                    for c in range(NC):
                        vpe = fin(te.matmul(ps_sw[:], onesc[:],
                                            wsb[:, c * D + half * 512:c * D + half * 512 + 512],
                                            start=(c == 0), stop=(c == NC - 1)), "pe")
                    sc.wait_ge(S["pe"], vpe)
                    vac = fin(sc.copy(out=swrow3[0:1, wi * D + half * 512:wi * D + half * 512 + 512],
                                      in_=ps_sw[:]), "ac")
                    te.wait_ge(S["ac"], vac)
            sw_scope.close()

            xsc_j = p1.enter_context(nc.sbuf_tensor("xsc_j", [128, 2 * NC * 128], BF16))
            qjit = p1.enter_context(nc.sbuf_tensor("qjit", [128, 256], BF16))
            kjit = p1.enter_context(nc.sbuf_tensor("kjit", [128, 2 * 256], BF16))
            v_tm = p1.enter_context(nc.sbuf_tensor("v_tm", [128, 2 * D], BF16))
            kg_tm = p1.enter_context(nc.sbuf_tensor("kg_tm", [128, 2 * 768], BF16))
            ktmp = p1.enter_context(nc.sbuf_tensor("ktmp", [128, 768], BF16))
            qtmp = p1.enter_context(nc.sbuf_tensor("qtmp", [128, 768], BF16))
            qsums = p1.enter_context(nc.sbuf_tensor("qsums", [128, 32], F32))
            qsm = qtmp
            qsmTj = p1.enter_context(nc.sbuf_tensor("qsmTj", [64, GHEADS * 128], BF16))
            ctxn = p1.enter_context(nc.sbuf_tensor("ctxn", [64, GHEADS * 64], BF16))
            attnT = p1.enter_context(nc.sbuf_tensor("attnT", [128, NC * T], BF16))
            smb = p1.enter_context(nc.sbuf_tensor("smb", [128, 256], F32))
            p16 = p1.enter_context(nc.sbuf_tensor("p16", [128, 256], BF16))
            ptb = p1.enter_context(nc.sbuf_tensor("ptb", [128, 256], BF16))
            nmx = p1.enter_context(nc.sbuf_tensor("nmx", [128, 8], F32))
            zrec = p1.enter_context(nc.sbuf_tensor("zrec", [64, 16], F32))

            tl = ExitStack()
            ps_proj = tl.enter_context(nc.psum_tensor("ps_proj", [128, 1024], F32))
            ps_S = tl.enter_context(nc.psum_tensor("ps_S", [64, 1024], F32))
            ps_sc = tl.enter_context(nc.psum_tensor("ps_sc", [128, 256], F32))
            ps_pv = tl.enter_context(nc.psum_tensor("ps_pv", [128, 512], F32))
            ps_pt = tl.enter_context(nc.psum_tensor("ps_pt", [128, 1536], BF16))

            def s_col(h):
                return 512 * (h // 6) + (h % 6) * 80

            ve.wait_ge(S["dv"], v_rbc1)
            if ATTNA_MEMSET:
                fin(ve.memset(ctxn[:], 0.0), "dv")

            v_evac_pt = 0
            v_evac_S = 0
            mr_row = mrneg

            for tau in range(ATTNA_TAUS):
                sl = tau % 2
                t0 = tau * 128

                ve.wait_ge(S["pe"], C["pe"])
                for c in range(NC):
                    vdv = fin(ve.tensor_mul(
                        out=xsc_j[:, (sl * NC + c) * 128:(sl * NC + c) * 128 + 128],
                        in0=stream[:, c * T + t0:c * T + t0 + 128],
                        in1=rbc[:, t0:t0 + 128]), "dv")
                v_xsc = vdv

                def xs(c):
                    return xsc_j[:, (sl * NC + c) * 128:(sl * NC + c) * 128 + 128]

                te.wait_ge(S["dv"], v_xsc)
                for oc in range(2):
                    for c in range(NC):
                        te.matmul(ps_proj[:, oc * 128:oc * 128 + 128],
                                  wq_sb[:, c * D + oc * 128:c * D + oc * 128 + 128],
                                  xs(c), start=(c == 0), stop=False)
                    fin(te.matmul(ps_proj[:, oc * 128:oc * 128 + 128],
                                  swrow3[0:1, oc * 128:oc * 128 + 128],
                                  mr_row[0:1, t0:t0 + 128], start=False, stop=True), "pe")
                for oc in range(2):
                    for c in range(NC):
                        te.matmul(ps_proj[:, 512 + oc * 128:512 + oc * 128 + 128],
                                  wk_sb[:, c * D + oc * 128:c * D + oc * 128 + 128],
                                  xs(c), start=(c == 0), stop=False)
                    vpe_k = fin(te.matmul(ps_proj[:, 512 + oc * 128:512 + oc * 128 + 128],
                                          swrow3[0:1, D + oc * 128:D + oc * 128 + 128],
                                          mr_row[0:1, t0:t0 + 128], start=False, stop=True), "pe")
                ve.wait_ge(S["pe"], vpe_k)
                fin(ve.tensor_copy(out=qjit[:], in_=ps_proj[:, 0:256]), "dv")
                v_qkloc = fin(ve.tensor_copy(out=kjit[:, sl * 256:sl * 256 + 256],
                                             in_=ps_proj[:, 512:768]), "dv")

                te.wait_ge(S["dv"], v_qkloc)
                for nc2 in range(2):
                    for c in range(NC):
                        te.matmul(ps_proj[:, nc2 * 512:nc2 * 512 + 512],
                                  xs(c), wv_sb[:, c * D + nc2 * 512:c * D + nc2 * 512 + 512],
                                  start=(c == 0), stop=False)
                    vpe = fin(te.matmul(ps_proj[:, nc2 * 512:nc2 * 512 + 512],
                                        mr_row[0:1, t0:t0 + 128],
                                        swrow3[0:1, 2 * D + nc2 * 512:2 * D + nc2 * 512 + 512],
                                        start=False, stop=True), "pe")
                ve.wait_ge(S["pe"], vpe)
                v_vtm = fin(ve.tensor_copy(out=v_tm[:, sl * D:sl * D + D],
                                           in_=ps_proj[:]), "dv")

                te.wait_ge(S["dv"], v_vtm)
                for nc2, w in ((0, 512), (1, 256)):
                    for c in range(NC):
                        te.matmul(ps_proj[:, nc2 * 512:nc2 * 512 + w],
                                  xs(c),
                                  wk_sb[:, c * D + 256 + nc2 * 512:c * D + 256 + nc2 * 512 + w],
                                  start=(c == 0), stop=False)
                    vpe = fin(te.matmul(ps_proj[:, nc2 * 512:nc2 * 512 + w],
                                        mr_row[0:1, t0:t0 + 128],
                                        swrow3[0:1, D + 256 + nc2 * 512:D + 256 + nc2 * 512 + w],
                                        start=False, stop=True), "pe")
                ve.wait_ge(S["pe"], vpe)
                vdv = fin(ve.tensor_scalar_min(out=ktmp[:], in0=ps_proj[:, 0:768],
                                               scalar1=0.0), "dv")
                sc.wait_ge(S["dv"], vdv)
                vac = fin(sc.activation(out=ktmp[:], in_=ktmp[:], func=AF.Exp), "ac")
                ve.wait_ge(S["ac"], vac)
                v_kg = fin(ve.scalar_tensor_tensor(out=kg_tm[:, sl * 768:sl * 768 + 768],
                                                   in0=ps_proj[:, 0:768], scalar=0.0,
                                                   in1=ktmp[:], op0=ALU.max, op1=ALU.add), "dv")

                te.wait_ge(S["dv"], v_kg)
                for nc2, w in ((0, 512), (1, 256)):
                    for c in range(NC):
                        te.matmul(ps_proj[:, nc2 * 512:nc2 * 512 + w],
                                  xs(c),
                                  wq_sb[:, c * D + 256 + nc2 * 512:c * D + 256 + nc2 * 512 + w],
                                  start=(c == 0), stop=False)
                    vpe = fin(te.matmul(ps_proj[:, nc2 * 512:nc2 * 512 + w],
                                        mr_row[0:1, t0:t0 + 128],
                                        swrow3[0:1, 256 + nc2 * 512:256 + nc2 * 512 + w],
                                        start=False, stop=True), "pe")
                sc.wait_ge(S["pe"], vpe)
                vac = fin(sc.activation(out=qtmp[:], in_=ps_proj[:, 0:768], func=AF.Exp), "ac")
                ve.wait_ge(S["ac"], vac)
                for h in range(GHEADS):
                    fin(ve.reduce_sum(out=qsums[:, h:h + 1],
                                      in_=qtmp[:, h * 64:h * 64 + 64], axis=AX.X), "dv")
                ve.engine_nop(); ve.engine_nop()
                fin(ve.reciprocal(out=qsums[:, 16:16 + GHEADS], in_=qsums[:, 0:GHEADS]), "dv")
                ve.engine_nop(); ve.engine_nop()
                for h in range(GHEADS):
                    vdv = fin(ve.tensor_scalar_mul(out=qsm[:, h * 64:h * 64 + 64],
                                                   in0=qtmp[:, h * 64:h * 64 + 64],
                                                   scalar1=qsums[:, 16 + h:17 + h]), "dv")
                v_qsm = vdv
                te.wait_ge(S["dv"], v_qsm)
                if tau > 0:
                    te.wait_ge(S["dv"], v_evac_pt)
                for h in range(GHEADS):
                    vpe = fin(te.transpose(ps_pt[0:64, h * 128:h * 128 + 128],
                                           qsm[:, h * 64:h * 64 + 64], ident[:]), "pe")
                ve.wait_ge(S["pe"], vpe)
                v_evac_pt = fin(ve.tensor_copy(out=qsmTj[:],
                                               in_=ps_pt[0:64, 0:GHEADS * 128]), "dv")

                v_sm_read = 0
                for h in range(LHEADS if ATTNA_LOCAL else 0):
                    pbase = (h % 2) * 64
                    cch = h // 2
                    lq = qjit[pbase:pbase + 64, cch * 128:cch * 128 + 128]
                    if v_sm_read:
                        te.wait_ge(S["dv"], v_sm_read)
                    if tau == 0:
                        if h == 0:
                            ve.wait_ge(S["pe"], C["pe"])
                            vdv0 = fin(ve.memset(ps_sc[:, 0:128], 0.0), "dv")
                            te.wait_ge(S["dv"], vdv0)
                        vpe = fin(te.matmul(ps_sc[:, 128:256], lq,
                                            kjit[pbase:pbase + 64,
                                                 sl * 256 + cch * 128:sl * 256 + cch * 128 + 128],
                                            start=(h > 0), stop=True), "pe")
                    else:
                        osl = (tau - 1) % 2
                        te.matmul(ps_sc[:, 0:128], lq,
                                  kjit[pbase:pbase + 64,
                                       osl * 256 + cch * 128:osl * 256 + cch * 128 + 128],
                                  start=True, stop=False)
                        vpe = fin(te.matmul(ps_sc[:, 128:256], lq,
                                            kjit[pbase:pbase + 64,
                                                 sl * 256 + cch * 128:sl * 256 + cch * 128 + 128],
                                            start=True, stop=True), "pe")
                    ve.wait_ge(S["pe"], vpe)
                    msk = mask0 if tau == 0 else maskN
                    v_sm_read = fin(ve.scalar_tensor_tensor(out=smb[:], in0=ps_sc[:],
                                                            scalar=0.125, in1=msk[:],
                                                            op0=ALU.mult, op1=ALU.add), "dv")
                    ve.engine_nop(); ve.engine_nop()
                    vdv = fin(ve.reduce_max(out=nmx[:, 0:1], in_=smb[:], axis=AX.X,
                                            negate=True), "dv")
                    sc.wait_ge(S["dv"], vdv)
                    vac = fin(sc.activation(out=p16[:], in_=smb[:], func=AF.Exp,
                                            bias=nmx[:, 0:1], accum_out=nmx[:, 1:2]), "ac")
                    ve.wait_ge(S["ac"], vac)
                    fin(ve.reciprocal(out=nmx[:, 2:3], in_=nmx[:, 1:2]), "dv")
                    ve.engine_nop(); ve.engine_nop()
                    vdv = fin(ve.tensor_scalar_mul(out=p16[:], in0=p16[:],
                                                   scalar1=nmx[:, 2:3]), "dv")
                    te.wait_ge(S["dv"], vdv)
                    if h == 0:
                        te.wait_ge(S["dv"], v_evac_pt)
                    te.transpose(ps_pt[:, 0:128], p16[:, 0:128], ident[:])
                    vpe = fin(te.transpose(ps_pt[:, 128:256], p16[:, 128:256], ident[:]), "pe")
                    ve.wait_ge(S["pe"], vpe)
                    vdv = fin(ve.tensor_copy(out=ptb[:], in_=ps_pt[:, 0:256]), "dv")
                    te.wait_ge(S["dv"], vdv)
                    pvslot = ps_pv[pbase:pbase + 64, cch * 128:cch * 128 + 128]
                    tp = (0, pbase)
                    started = False
                    if tau > 0:
                        te.matmul(pvslot,
                                  v_tm[:, ((tau - 1) % 2) * D + h * 64:((tau - 1) % 2) * D + h * 64 + 64],
                                  ptb[:, 0:128], start=(h == 0), stop=False, tile_position=tp)
                        started = True
                    vpe_pv = fin(te.matmul(pvslot,
                                           v_tm[:, sl * D + h * 64:sl * D + h * 64 + 64],
                                           ptb[:, 128:256],
                                           start=(h == 0 and not started), stop=True,
                                           tile_position=tp), "pe")
                if ATTNA_LOCAL:
                    ve.wait_ge(S["pe"], vpe_pv)
                    o_ap = attnT[:].rearrange("p (c t) -> p c t", t=T)[:, 0:2, t0:t0 + 128]
                    i_ap = ps_pv[:, 0:256].rearrange("p (c t) -> p c t", t=128)
                    v_locevac = fin(ve.tensor_copy(out=o_ap, in_=i_ap), "dv")
                else:
                    v_locevac = 0

                for ui in range(2 if ATTNA_GLOBAL else 0):
                    u = 2 * tau + ui
                    ub = ui * 64
                    te.wait_ge(S["dv"], max(v_evac_S, v_locevac))
                    for h in range(GHEADS):
                        vpe = fin(te.matmul(
                            ps_pv[(h % 2) * 64:(h % 2) * 64 + 64,
                                  (h // 2) * 64:(h // 2) * 64 + 64],
                            ctxn[:, h * 64:h * 64 + 64],
                            qsmTj[:, h * 128 + ub:h * 128 + ub + 64],
                            start=True, stop=True, tile_position=(0, (h % 2) * 64)), "pe")
                    ve.wait_ge(S["pe"], vpe)
                    o_ap = attnT[:].rearrange("p (c t) -> p c t", t=T)[:, 2:8, u * 64:u * 64 + 64]
                    i_ap = ps_pv[:, 0:384].rearrange("p (c t) -> p c t", t=64)
                    v_locevac = fin(ve.tensor_copy(out=o_ap, in_=i_ap), "dv")
                    te.wait_ge(S["dv"], v_locevac)
                    for h in range(GHEADS):
                        co = s_col(h)
                        stt = (u == 0 and h in (0, 6))
                        te.matmul(ps_S[:, co:co + 64],
                                  kg_tm[ub:ub + 64, sl * 768 + h * 64:sl * 768 + h * 64 + 64],
                                  v_tm[ub:ub + 64, sl * D + 256 + h * 64:sl * D + 256 + h * 64 + 64],
                                  start=stt, stop=(u == NU - 1), tile_position=(ub, 0))
                        vpe = fin(te.matmul(ps_S[:, co + 64:co + 65],
                                            kg_tm[ub:ub + 64, sl * 768 + h * 64:sl * 768 + h * 64 + 64],
                                            onesc[ub:ub + 64, :],
                                            start=False, stop=(u == NU - 1),
                                            tile_position=(ub, 0)), "pe")
                    ve.wait_ge(S["pe"], vpe)
                    fin(ve.tensor_scalar_add(out=zrec[:, 0:6],
                                             in0=ps_S[0:64, 64:465:80], scalar1=1e-6), "dv")
                    fin(ve.tensor_scalar_add(out=zrec[:, 6:12],
                                             in0=ps_S[0:64, 576:977:80], scalar1=1e-6), "dv")
                    ve.engine_nop(); ve.engine_nop()
                    fin(ve.reciprocal(out=zrec[:, 0:12], in_=zrec[:, 0:12]), "dv")
                    ve.engine_nop(); ve.engine_nop()
                    for h in range(GHEADS):
                        co = s_col(h)
                        vdv = fin(ve.tensor_scalar_mul(out=ctxn[:, h * 64:h * 64 + 64],
                                                       in0=ps_S[0:64, co:co + 64],
                                                       scalar1=zrec[:, h:h + 1]), "dv")
                    v_evac_S = vdv
            tl.close()

            gp.wait_ge(S["pe"], C["pe"])
            gp.wait_ge(S["dv"], C["dv"])
            v_wo = 0
            for c in range(NC if ATTNA_WOLOADS else 0):
                v_wo = ldg(wq_sb[:, c * D:(c + 1) * D], aWo[c * 128:(c + 1) * 128, :])
            wo_scope = ExitStack()
            ps_wo = wo_scope.enter_context(nc.psum_tensor("ps_wo", [128, 1024], F32))
            if v_wo:
                te.wait_ge(S["gp"], v_wo)
            te.wait_ge(S["dv"], C["dv"])
            v_slotfree = {0: 0, 1: 0}
            v_addflush = {0: 0, 1: 0}
            for oc in range(NC if ATTNA_WO else 0):
                for nc4 in range(4):
                    psl = nc4 % 2
                    pslot = ps_wo[:, psl * 512:psl * 512 + 512]
                    if v_addflush[psl]:
                        te.wait_ge(S["dv"], v_addflush[psl])
                    for c in range(NC):
                        te.matmul(pslot,
                                  wq_sb[:, c * D + oc * 128:c * D + oc * 128 + 128],
                                  attnT[:, c * T + nc4 * 512:c * T + nc4 * 512 + 512],
                                  start=(c == 0), stop=False)
                    vpe = fin(te.matmul(pslot, borow[0:1, oc * 128:oc * 128 + 128],
                                        onesr[:, 0:512],
                                        start=False, stop=True), "pe")
                    ve.wait_ge(S["pe"], vpe)
                    v_slotfree[psl] = fin(ve.tensor_add(
                        out=stream[:, oc * T + nc4 * 512:oc * T + nc4 * 512 + 512],
                        in0=pslot,
                        in1=stream[:, oc * T + nc4 * 512:oc * T + nc4 * 512 + 512]), "dv")
                    v_addflush[psl] = fin(ve.tensor_copy(out=flushb[0:1, 0:4],
                                                         in_=flushb[0:1, 4:8]), "dv")
            if ATTNA_WO:
                te.wait_ge(S["dv"], max(v_addflush.values()))
            wo_scope.close()
        p1.close()

        for eng in (sy, ve, sc, te, gp):
            eng.wait_ge(S["dv"], C["dv"])
            eng.wait_ge(S["pe"], C["pe"])
            eng.wait_ge(S["ac"], C["ac"])
            eng.wait_ge(S["ld"], C["ld"])
            eng.wait_ge(S["gp"], C["gp"])
        p2 = ExitStack()
        swg_row = p2.enter_context(nc.sbuf_tensor("swg_row", [1, HID], BF16))
        swu_row = p2.enter_context(nc.sbuf_tensor("swu_row", [1, HID], BF16))

        if DO_FFN1 or DO_FFN2:
            swf_scope = ExitStack()
            ps_sw3_pad = [swf_scope.enter_context(nc.psum_tensor("ps_sw3p%d" % i, [128, 512], F32))
                          for i in range(7)]
            ps_sw3 = swf_scope.enter_context(nc.psum_tensor("ps_sw3", [1, 512], F32))
            swslab = p2.enter_context(nc.sbuf_tensor("swslab", [128, 4096], BF16))
            for src_sl, prow in ((wg_sl, swg_row), (wu_sl, swu_row)):
                for s6 in range(6):
                    sy.wait_ge(S["ld"], v_conv_all)
                    sy.wait_ge(S["pe"], C["pe"])
                    vld = ld(swslab[:], src_sl[:, s6 * 4096:(s6 + 1) * 4096])
                    te.wait_ge(S["ld"], vld)
                    for jci in range(4):
                        for c in range(NC):
                            vpe = fin(te.matmul(ps_sw3[:, 0:128], onesc[:],
                                                swslab[:, c * 512 + jci * 128:c * 512 + jci * 128 + 128],
                                                start=(c == 0), stop=(c == NC - 1)), "pe")
                        sc.wait_ge(S["pe"], vpe)
                        fin(sc.copy(out=prow[0:1, (s6 * 4 + jci) * 128:(s6 * 4 + jci) * 128 + 128],
                                    in_=ps_sw3[:, 0:128]), "ac")
                        te.wait_ge(S["ac"], C["ac"])
            swf_scope.close()

        def emit_ffn_pass(tag):
            fs = ExitStack()
            wgsl = fs.enter_context(nc.sbuf_tensor("wgsl" + tag, [128, 2 * 4096], BF16))
            wusl = fs.enter_context(nc.sbuf_tensor("wusl" + tag, [128, 2 * 4096], BF16))
            wdsl = fs.enter_context(nc.sbuf_tensor("wdsl" + tag, [128, 2 * 4096], BF16))
            h_sb = fs.enter_context(nc.sbuf_tensor("h_sb" + tag, [128, NJC * 512], BF16))
            xscg = fs.enter_context(nc.sbuf_tensor("xscg" + tag, [128, NC * 512], BF16))
            gsil = fs.enter_context(nc.sbuf_tensor("gsil" + tag, [128, 1024], BF16))
            sy.wait_ge(S["pe"], C["pe"])
            sy.wait_ge(S["dv"], C["dv"])
            sy.wait_ge(S["ac"], C["ac"])
            slab_pe_use = {0: 0, 1: 0}
            dslab_pe_use = {0: 0, 1: 0}

            def emit_xscg(gg0, v_rbc, pe_val):
                ve.wait_ge(S["dv"], v_rbc)
                ve.wait_ge(S["pe"], pe_val)
                vd = 0
                for c in range(NC):
                    vd = fin(ve.tensor_mul(out=xscg[:, c * 512:(c + 1) * 512],
                                           in0=stream[:, c * T + gg0:c * T + gg0 + 512],
                                           in1=rbc[:, gg0:gg0 + 512]), "dv")
                return vd

            v_rbc0 = emit_ln(0, 512)
            v_xscg = emit_xscg(0, v_rbc0, C["pe"])
            v_dve_n = v_mr_n = v_rbc_n = v_s0 = v_s1 = 0
            for g in range(4):
                g0 = g * 512
                guscope = ExitStack()
                ps_g = [guscope.enter_context(nc.psum_tensor("ps_g%s%d_%d" % (tag, g, i),
                                                             [128, 512], F32)) for i in range(2)]
                ps_u = [guscope.enter_context(nc.psum_tensor("ps_u%s%d_%d" % (tag, g, i),
                                                             [128, 512], F32)) for i in range(2)]
                te.wait_ge(S["dv"], v_xscg)
                v_gfree = {0: 0, 1: 0}
                v_ufree = {0: 0, 1: 0}
                v_slab = {0: 0, 1: 0}
                for jc in range(NJC):
                    s6, jci = jc // 4, jc % 4
                    sl = s6 % 2
                    if jc == 2 and g < 3:
                        lnscope, ps_sN, ps_qN = emit_ln_psum(512)
                        v_dve_n = emit_ln_dve(g0 + 512, 512, 0)
                    if jc == 6 and g < 3:
                        v_s0 = emit_ln_stats_wave(ps_sN, ps_qN, 512, 0, v_dve_n)
                        v_dve_n = emit_ln_dve(g0 + 512, 512, 1)
                    if jc == 10 and g < 3:
                        v_s1 = emit_ln_stats_wave(ps_sN, ps_qN, 512, 1, v_dve_n)
                        v_mr_n = emit_ln_tail(ps_sN, ps_qN, lnscope, g0 + 512, 512, v_s1)
                    if jc == 14 and g < 3:
                        v_rbc_n = emit_ln_bc(g0 + 512, 512, v_mr_n)
                    if jci == 0:
                        sy.wait_ge(S["ld"], v_conv_all)
                        sy.wait_ge(S["pe"], slab_pe_use[sl])
                        ld(wgsl[:, sl * 4096:(sl + 1) * 4096],
                           wg_sl[:, s6 * 4096:(s6 + 1) * 4096])
                        v_slab[sl] = ld(wusl[:, sl * 4096:(sl + 1) * 4096],
                                        wu_sl[:, s6 * 4096:(s6 + 1) * 4096])
                        te.wait_ge(S["ld"], v_slab[sl])
                    psl = jc % 2
                    if v_gfree[psl]:
                        te.wait_ge(S["ac"], v_gfree[psl])
                    for c in range(NC):
                        te.matmul(ps_g[psl][:],
                                  wgsl[:, sl * 4096 + c * 512 + jci * 128:sl * 4096 + c * 512 + jci * 128 + 128],
                                  xscg[:, c * 512:(c + 1) * 512], start=(c == 0), stop=False)
                    vpe_g = fin(te.matmul(ps_g[psl][:], swg_row[0:1, jc * 128:jc * 128 + 128],
                                          mrneg[0:1, g0:g0 + 512], start=False, stop=True), "pe")
                    if v_ufree[psl]:
                        te.wait_ge(S["dv"], v_ufree[psl])
                    for c in range(NC):
                        te.matmul(ps_u[psl][:],
                                  wusl[:, sl * 4096 + c * 512 + jci * 128:sl * 4096 + c * 512 + jci * 128 + 128],
                                  xscg[:, c * 512:(c + 1) * 512], start=(c == 0), stop=False)
                    vpe_u = fin(te.matmul(ps_u[psl][:], swu_row[0:1, jc * 128:jc * 128 + 128],
                                          mrneg[0:1, g0:g0 + 512], start=False, stop=True), "pe")
                    if jci == 3:
                        slab_pe_use[sl] = vpe_u
                    sc.wait_ge(S["pe"], vpe_g)
                    v_gfree[psl] = fin(sc.activation(out=gsil[:, psl * 512:psl * 512 + 512],
                                                     in_=ps_g[psl][:], func=AF.Silu,
                                                     scale=INV127), "ac")
                    ve.wait_ge(S["ac"], v_gfree[psl])
                    ve.wait_ge(S["pe"], vpe_u)
                    fin(ve.scalar_tensor_tensor(
                        out=h_sb[:, jc * 512:(jc + 1) * 512],
                        in0=ps_u[psl][:], scalar=INV127, in1=gsil[:, psl * 512:psl * 512 + 512],
                        op0=ALU.mult, op1=ALU.mult), "dv")
                    v_ufree[psl] = fin(ve.tensor_copy(out=flushb[0:1, 0:4],
                                                      in_=flushb[0:1, 4:8]), "dv")
                v_ufree_max = max(v_ufree.values())
                te.wait_ge(S["dv"], v_ufree_max)
                guscope.close()
                if g < 3:
                    v_xscg = emit_xscg(g0 + 512, v_rbc_n, vpe_u)
                dscope = ExitStack()
                ps_d = [dscope.enter_context(
                    nc.psum_tensor("ps_d%s%d_%d" % (tag, g, dc), [128, 512], F32))
                    for dc in range(8)]
                for jc in range(NJC):
                    s6, jci = jc // 4, jc % 4
                    sl = s6 % 2
                    if jci == 0:
                        sy.wait_ge(S["pe"], dslab_pe_use[sl])
                        vws = ld(wdsl[:, sl * 4096:(sl + 1) * 4096],
                                 wd_sl[:, s6 * 4 * 1024:(s6 * 4 + 4) * 1024])
                        te.wait_ge(S["ld"], vws)
                    for dc in range(8):
                        vpe = fin(te.matmul(
                            ps_d[dc][:],
                            wdsl[:, sl * 4096 + jci * 1024 + dc * 128:sl * 4096 + jci * 1024 + dc * 128 + 128],
                            h_sb[:, jc * 512:(jc + 1) * 512],
                            start=(jc == 0), stop=(jc == NJC - 1)), "pe")
                    if jci == 3:
                        dslab_pe_use[sl] = vpe
                ve.wait_ge(S["pe"], vpe)
                for dc in range(8):
                    vdv = fin(ve.scalar_tensor_tensor(
                        out=stream[:, dc * T + g0:dc * T + g0 + 512],
                        in0=ps_d[dc][:], scalar=INV127,
                        in1=stream[:, dc * T + g0:dc * T + g0 + 512],
                        op0=ALU.mult, op1=ALU.add), "dv")
                te.wait_ge(S["dv"], vdv)
                dscope.close()
            fs.close()

        if DO_FFN1:
            emit_ffn_pass("f1")

        if DO_ATTN_B:
            ab = ExitStack()
            wqb = ab.enter_context(nc.sbuf_tensor("wqb", [128, NC * D], BF16))
            wkb = ab.enter_context(nc.sbuf_tensor("wkb", [128, NC * D], BF16))
            wvb = ab.enter_context(nc.sbuf_tensor("wvb", [128, NC * D], BF16))
            swrow3b = ab.enter_context(nc.sbuf_tensor("swrow3b", [1, 3 * D], BF16))
            bqkvrow = ab.enter_context(nc.sbuf_tensor("bqkvrow", [1, 3 * D], BF16))
            gp.wait_ge(S["pe"], C["pe"])
            gp.wait_ge(S["dv"], C["dv"])
            gp.wait_ge(S["ac"], C["ac"])
            v_wb = 0
            for wsb, src in ((wqb, bWq), (wkb, bWk), (wvb, bWv)):
                for c in range(NC):
                    v_wb = ldg(wsb[:, c * D:(c + 1) * D], src[c * 128:(c + 1) * 128, :])
            ldg(bqkvrow[0:1, 0:D], bbq)
            ldg(bqkvrow[0:1, D:2 * D], bbk)
            v_wb = ldg(bqkvrow[0:1, 2 * D:3 * D], bbv)

            tb_scope = ExitStack()
            ps_tb = tb_scope.enter_context(nc.psum_tensor("ps_tb", [128, 16], F32))
            te.wait_ge(S["gp"], v_temp)
            vpe = fin(te.matmul(ps_tb[:], onesr[0:1, 0:128], temprow[0:1, :],
                                start=True, stop=True), "pe")
            ve.wait_ge(S["pe"], vpe)
            v_tempbc = fin(ve.tensor_copy(out=tempbc[:], in_=ps_tb[:]), "dv")
            tb_scope.close()

            swb_scope = ExitStack()
            ps_sw2 = swb_scope.enter_context(nc.psum_tensor("ps_sw2", [1, 512], F32))
            te.wait_ge(S["gp"], v_wb)
            te.wait_ge(S["dv"], v_tempbc)
            for wi, wsb in enumerate([wqb, wkb, wvb]):
                for half in range(2):
                    for c in range(NC):
                        vpe = fin(te.matmul(ps_sw2[:], onesc[:],
                                            wsb[:, c * D + half * 512:c * D + half * 512 + 512],
                                            start=(c == 0), stop=(c == NC - 1)), "pe")
                    sc.wait_ge(S["pe"], vpe)
                    vac = fin(sc.copy(out=swrow3b[0:1, wi * D + half * 512:wi * D + half * 512 + 512],
                                      in_=ps_sw2[:]), "ac")
                    te.wait_ge(S["ac"], vac)
            swb_scope.close()

            xsch = ab.enter_context(nc.sbuf_tensor("xsch", [128, NC * 128], BF16))
            yq_sb = ab.enter_context(nc.sbuf_tensor("yq_sb", [128, NC * 128], BF16))
            k_sb = ab.enter_context(nc.sbuf_tensor("k_sb", [64, T], BF16))
            vv_sb = ab.enter_context(nc.sbuf_tensor("vv_sb", [64, T], BF16))
            pP = [ab.enter_context(nc.sbuf_tensor("pP%d" % i, [128, T], BF16)) for i in range(2)]
            obuf = ab.enter_context(nc.sbuf_tensor("obuf", [64, T], BF16))
            sqt = ab.enter_context(nc.sbuf_tensor("sqt", [64, T], BF16))
            srow = ab.enter_context(nc.sbuf_tensor("srow", [64, 8], F32))
            wg16 = ab.enter_context(nc.sbuf_tensor("wg16", [64, 2], BF16))
            biasc = ab.enter_context(nc.sbuf_tensor("biasc", [128, 16], F32))
            sigt = ab.enter_context(nc.sbuf_tensor("sigt", [128, 16], F32))
            qs_sb = [ab.enter_context(nc.sbuf_tensor("qs%d" % i, [128, 64], BF16)) for i in range(2)]
            rsb = ab.enter_context(nc.sbuf_tensor("rsb", [128, 8], F32))

            def emit_sig_block(h):
                h0 = h * 128
                fin(ve.reduce_sum(out=srow[:, 0:1], in_=vv_sb[:], axis=AX.X), "dv")
                fin(ve.reduce_sum(out=srow[:, 1:2], in_=sqt[:], axis=AX.X), "dv")
                ve.engine_nop(); ve.engine_nop()
                fin(ve.tensor_scalar_mul(out=srow[:, 2:3], in0=srow[:, 0:1], scalar1=1.0 / T), "dv")
                ve.engine_nop(); ve.engine_nop()
                fin(ve.tensor_mul(out=srow[:, 3:4], in0=srow[:, 2:3], in1=srow[:, 2:3]), "dv")
                fin(ve.tensor_scalar_mul(out=srow[:, 4:5], in0=srow[:, 1:2], scalar1=1.0 / T), "dv")
                ve.engine_nop(); ve.engine_nop()
                fin(ve.tensor_sub(out=srow[:, 4:5], in0=srow[:, 4:5], in1=srow[:, 3:4]), "dv")
                ve.engine_nop(); ve.engine_nop()
                fin(ve.tensor_scalar_max(out=srow[:, 4:5], in0=srow[:, 4:5], scalar1=0.0), "dv")
                ve.engine_nop(); ve.engine_nop()
                fin(ve.tensor_copy(out=wg16[:, 0:1], in_=srow[:, 2:3]), "dv")
                fin(ve.tensor_copy(out=wg16[:, 1:2], in_=srow[:, 4:5]), "dv")
                vdv_k = fin(ve.tensor_mul(out=sqt[:], in0=k_sb[:], in1=k_sb[:]), "dv")
                sgscope = ExitStack()
                ps_sig = sgscope.enter_context(nc.psum_tensor("ps_sig%d" % h, [128, 32], F32))
                te.wait_ge(S["dv"], vdv_k)
                for tt in range(16):
                    te.matmul(ps_sig[:, tt:tt + 1], k_sb[:, tt * 128:(tt + 1) * 128],
                              wg16[:, 0:1], start=True, stop=True)
                    vpe = fin(te.matmul(ps_sig[:, 16 + tt:17 + tt], sqt[:, tt * 128:(tt + 1) * 128],
                                        wg16[:, 1:2], start=True, stop=True), "pe")
                ve.wait_ge(S["pe"], vpe)
                vdv = fin(ve.tensor_scalar_add(out=biasc[:], in0=ps_sig[:, 16:32],
                                               scalar1=1e-20), "dv")
                sc.wait_ge(S["dv"], vdv)
                fin(sc.activation(out=sigt[:], in_=biasc[:], func=AF.Ln), "ac")
                sc.nop(); sc.nop()
                vac = fin(sc.activation(out=sigt[:], in_=sigt[:], func=AF.Exp, scale=0.5), "ac")
                ve.wait_ge(S["ac"], vac)
                vdv = fin(ve.scalar_tensor_tensor(out=biasc[:], in0=sigt[:], scalar=-4.0,
                                                  in1=ps_sig[:, 0:16], op0=ALU.mult,
                                                  op1=ALU.subtract), "dv")
                te.wait_ge(S["dv"], vdv)
                sgscope.close()
            for h in range(ATTNB_HEADS):
                h0 = h * 128
                v_rbch = emit_ln(h0, 128)
                ve.wait_ge(S["dv"], v_rbch)
                ve.wait_ge(S["pe"], C["pe"])
                for c in range(NC):
                    vdv = fin(ve.tensor_mul(out=xsch[:, c * 128:(c + 1) * 128],
                                            in0=stream[:, c * T + h0:c * T + h0 + 128],
                                            in1=rbc[:, h0:h0 + 128]), "dv")
                v_xsch = vdv

                prscope = ExitStack()
                ps_yq = prscope.enter_context(nc.psum_tensor("ps_yq%d" % h, [128, 128], F32))
                ps_kv = prscope.enter_context(nc.psum_tensor("ps_kv%d" % h, [64, 512], F32))
                te.wait_ge(S["dv"], v_xsch)
                v_evac = 0
                for oc in range(NC if ATTNB_PROJ else 0):
                    if v_evac:
                        te.wait_ge(S["dv"], v_evac)
                    for c in range(NC):
                        te.matmul(ps_yq[:], wqb[:, c * D + oc * 128:c * D + oc * 128 + 128],
                                  xsch[:, c * 128:(c + 1) * 128], start=(c == 0), stop=False)
                    te.matmul(ps_yq[:], swrow3b[0:1, oc * 128:oc * 128 + 128],
                              mrneg[0:1, h0:h0 + 128], start=False, stop=False)
                    vpe = fin(te.matmul(ps_yq[:], bqkvrow[0:1, oc * 128:oc * 128 + 128],
                                        onesr[:, 0:128], start=False, stop=True), "pe")
                    ve.wait_ge(S["pe"], vpe)
                    v_evac = fin(ve.tensor_copy(out=yq_sb[:, oc * 128:(oc + 1) * 128],
                                                in_=ps_yq[:]), "dv")
                for mi, (wsb, dst, biasoff, swoff) in enumerate(
                        (((wkb, k_sb, D, D), (wvb, vv_sb, 2 * D, 2 * D)) if ATTNB_PROJ else ())):
                    for c01 in range(2):
                        for nc2 in range(2):
                            if v_evac:
                                te.wait_ge(S["dv"], v_evac)
                            for c in range(NC):
                                te.matmul(ps_kv[:],
                                          xsch[:, c * 128 + c01:c * 128 + c01 + 127:2],
                                          wsb[:, c * D + nc2 * 512:c * D + nc2 * 512 + 512],
                                          start=(c == 0), stop=False)
                            te.matmul(ps_kv[:], mrneg[0:1, h0 + c01:h0 + c01 + 127:2],
                                      swrow3b[0:1, swoff + nc2 * 512:swoff + nc2 * 512 + 512],
                                      start=False, stop=False)
                            vpe = fin(te.matmul(ps_kv[:], onesr[0:1, 0:64],
                                                bqkvrow[0:1, biasoff + nc2 * 512:biasoff + nc2 * 512 + 512],
                                                start=False, stop=True), "pe")
                            ve.wait_ge(S["pe"], vpe)
                            dcols = c01 * 1024 + nc2 * 512
                            if mi == 0:
                                v_evac = fin(ve.tensor_copy(out=dst[:, dcols:dcols + 512],
                                                            in_=ps_kv[:]), "dv")
                            else:
                                v_evac = fin(ve.tensor_scalar_mul(
                                    out=dst[:, dcols:dcols + 512], in0=ps_kv[:],
                                    scalar1=tempbc[0:64, h:h + 1]), "dv")
                if not ATTNB_SIG:
                    ve.wait_ge(S["pe"], C["pe"])
                    vdvz = fin(ve.memset(biasc[:], 0.0), "dv")
                    te.wait_ge(S["dv"], vdvz)
                if ATTNB_SIG:
                    ve.engine_nop(); ve.engine_nop()
                    fin(ve.tensor_mul(out=sqt[:], in0=vv_sb[:], in1=vv_sb[:]), "dv")
                    ve.engine_nop(); ve.engine_nop()
                if ATTNB_SIG:
                    emit_sig_block(h)
                prscope.close()

                aoscope = ExitStack()
                ps_A = aoscope.enter_context(nc.psum_tensor("ps_A%d" % h, [128, T], F32))
                ps_O = aoscope.enter_context(nc.psum_tensor("ps_O%d" % h, [64, T], F32))
                v_exp = [0, 0]
                v_qs = [0, 0]
                vpe2 = 0
                for tt in range(17 if ATTNB_CORE else 0):
                    if tt < 16:
                        if tt > 0:
                            te.wait_ge(S["ac"], v_exp[(tt - 1) % 2])
                        for ncs in range(4):
                            vpe1 = fin(te.matmul(ps_A[:, ncs * 512:(ncs + 1) * 512],
                                                 k_sb[:, tt * 128:(tt + 1) * 128],
                                                 vv_sb[:, ncs * 512:(ncs + 1) * 512],
                                                 start=True, stop=True), "pe")
                    if tt > 0:
                        pt = tt - 1
                        te.wait_ge(S["dv"], v_qs[pt % 2])
                        te.wait_ge(S["ac"], v_exp[pt % 2])
                        for ncs in range(4):
                            vpe2 = fin(te.matmul(ps_O[:, ncs * 512:(ncs + 1) * 512],
                                                 qs_sb[pt % 2][:],
                                                 pP[pt % 2][:, ncs * 512:(ncs + 1) * 512],
                                                 start=(pt == 0), stop=(pt == 15)), "pe")
                    if tt < 16:
                        sl2 = tt % 2
                        sc.wait_ge(S["pe"], vpe1)
                        v_exp[sl2] = fin(sc.activation(out=pP[sl2][:], in_=ps_A[:],
                                                       func=AF.Exp, bias=biasc[:, tt:tt + 1],
                                                       accum_out=rsb[:, sl2:sl2 + 1]), "ac")
                        ve.wait_ge(S["ac"], v_exp[sl2])
                        fin(ve.reciprocal(out=rsb[:, 2 + sl2:3 + sl2], in_=rsb[:, sl2:sl2 + 1]), "dv")
                        ve.engine_nop(); ve.engine_nop()
                        qcol = (tt % 8) * 128 + (1 if tt >= 8 else 0)
                        v_qs[sl2] = fin(ve.tensor_scalar_mul(
                            out=qs_sb[sl2][:],
                            in0=yq_sb[:, qcol:qcol + 127:2],
                            scalar1=rsb[:, 2 + sl2:3 + sl2]), "dv")
                if ATTNB_CORE:
                    ve.wait_ge(S["pe"], vpe2)
                    v_ob = fin(ve.tensor_copy(out=obuf[:], in_=ps_O[:]), "dv")
                    te.wait_ge(S["dv"], v_ob)
                aoscope.close()
                tscope = ExitStack()
                ps_t2 = tscope.enter_context(nc.psum_tensor("ps_t2%d" % h, [128, 2048], BF16))
                v_tadd = {0: 0, 1: 0}
                for b in range(16 if ATTNB_OUT else 0):
                    c01, sg = b // 8, b % 8
                    bsl = b % 2
                    slot = ps_t2[:, bsl * 1024:bsl * 1024 + 64]
                    if v_tadd[bsl]:
                        te.wait_ge(S["dv"], v_tadd[bsl])
                    vpe = fin(te.transpose(slot,
                                           obuf[:, c01 * 1024 + sg * 128:c01 * 1024 + sg * 128 + 128],
                                           ident[0:64, 0:64]), "pe")
                    ve.wait_ge(S["pe"], vpe)
                    ocols0 = sg * T + h0 + c01
                    v_tadd[bsl] = fin(ve.tensor_add(
                        out=stream[:, ocols0:ocols0 + 127:2],
                        in0=slot,
                        in1=stream[:, ocols0:ocols0 + 127:2]), "dv")
                if ATTNB_OUT:
                    te.wait_ge(S["dv"], max(v_tadd.values()))
                tscope.close()
            ab.close()

        if DO_FFN2:
            emit_ffn_pass("f2")
        p2.close()

        sy.wait_ge(S["dv"], C["dv"])
        sy.wait_ge(S["pe"], C["pe"])
        v_st = 0
        for c in range(NC):
            v_st = fin(sy.dma_start(out=yT[c * 128:(c + 1) * 128, :],
                                    in_=stream[:, c * T:(c + 1) * T]), "st", 16)
        sy.wait_ge(S["st"], v_st)

        blk.sync(lambda e: _replay(e, PROG["sy"]))
        blk.vector(lambda e: _replay(e, PROG["ve"]))
        blk.scalar(lambda e: _replay(e, PROG["sc"]))
        blk.tensor(lambda e: _replay(e, PROG["te"]))
        blk.gpsimd(lambda e: _replay(e, PROG["gp"]))

    nc.finalize()
    return nc


_NC_CACHE = None


def kernel(**inputs):
    global _NC_CACHE
    x = np.ascontiguousarray(np.asarray(inputs["L0_attn_a_x"], dtype=np.float32))
    B = x.shape[0]
    shared = {
        "aWq": np.ascontiguousarray(np.asarray(inputs["L0_attn_a_Wq"], np.float32)),
        "aWk": np.ascontiguousarray(np.asarray(inputs["L0_attn_a_Wk"], np.float32)),
        "aWv": np.ascontiguousarray(np.asarray(inputs["L0_attn_a_Wv"], np.float32)),
        "aWo": np.ascontiguousarray(np.asarray(inputs["L0_attn_a_Wo"], np.float32)),
        "abo": np.asarray(inputs["L0_attn_a_bo"], np.float32).reshape(1, D),
        "bWq": np.ascontiguousarray(np.asarray(inputs["L0_attn_b_Wq"], np.float32)),
        "bWk": np.ascontiguousarray(np.asarray(inputs["L0_attn_b_Wk"], np.float32)),
        "bWv": np.ascontiguousarray(np.asarray(inputs["L0_attn_b_Wv"], np.float32)),
        "bbq": np.asarray(inputs["L0_attn_b_bq"], np.float32).reshape(1, D),
        "bbk": np.asarray(inputs["L0_attn_b_bk"], np.float32).reshape(1, D),
        "bbv": np.asarray(inputs["L0_attn_b_bv"], np.float32).reshape(1, D),
        "btemp": np.asarray(inputs["L0_attn_b_temperature"], np.float32).reshape(1, 16),
        "wgT": np.ascontiguousarray(np.asarray(inputs["L0_ffn_w_gate"], np.int32).T),
        "wuT": np.ascontiguousarray(np.asarray(inputs["L0_ffn_w_up"], np.int32).T),
        "wdT": np.ascontiguousarray(np.asarray(inputs["L0_ffn_w_down"], np.int32).T),
        "ident": np.eye(128, dtype=BF),
        "onesc": np.ones((128, 1), dtype=BF),
        "onesr": np.ones((1, 512), dtype=BF),
        "crow": np.array([[EPS, 1e-20, -0.5, 0.5, INV127, 1.0 / D, 0.0, 0.0]], np.float32),
    }
    maskN = np.zeros((128, 256), np.float32)
    for i in range(128):
        maskN[i, 129 + i:] = -1e9
    mask0 = maskN.copy()
    mask0[:, :128] = -1e9
    shared["maskN"] = maskN.astype(BF)
    shared["mask0"] = mask0.astype(BF)

    if _NC_CACHE is None:
        _NC_CACHE = _build_nc()
    nc = _NC_CACHE
    in_maps = []
    for c in range(B):
        m = dict(shared)
        m["xT"] = np.ascontiguousarray(x[c].T)
        in_maps.append(m)
    res = run_bass_kernel_spmd(nc, in_maps, core_ids=list(range(B)))
    out = np.stack([np.ascontiguousarray(np.asarray(r["yT"]).T) for r in res.results])
    return out.astype(np.float32)
